# version 11
# speedup vs baseline: 1.1473x; 1.0044x over previous
import numpy as np
import concourse.bass as bass
import concourse.mybir as mybir
from concourse.bass_utils import run_bass_kernel_spmd

F32 = mybir.dt.float32
BF16 = mybir.dt.bfloat16
AF = mybir.ActivationFunctionType
ALU = mybir.AluOpType

H, P, G, N, CS, KCONV = 128, 64, 8, 128, 128, 4
D = 4096
INTER = 8192
GN = G * N
CONVD = INTER + 2 * GN
NPROJ = INTER + CONVD + H
NCT = NPROJ // 128
NDB = D // 128
EPS = 1e-5
FFH = 11008
FFQ = 2752
ZP = 1152
HP = 2944
ZKB = ZP // 128
HKB = HP // 128
SEGT = 22
_LAST_PROG = None


class Rec:
    __slots__ = ("fn", "deps", "sem", "amt", "n")

    def __init__(self, fn, deps, sem, amt):
        self.fn, self.deps, self.sem, self.amt = fn, deps, sem, amt


class DSem:

    def __init__(self, prog, name):
        self.sem = prog.nc.semaphore(name).__enter__()
        self.cnt = 0

    def tok(self):
        return (self.sem, self.cnt)


class Prog:
    ENGINES = ("pe", "act", "dve", "pool", "sp")

    SKIP = ("skip", 0)

    def __init__(self, nc):
        self.nc = nc
        self.recs = {e: [] for e in self.ENGINES}
        self.esem = {e: DSem(self, f"es_{e}") for e in self.ENGINES}
        self._dsems = []
        self.muted = False

    def dsem(self, name):
        s = DSem(self, name)
        self._dsems.append(s)
        return s

    def barrier(self):
        toks = [(es.sem, es.cnt) for es in self.esem.values() if es.cnt > 0]
        toks += [(ds.sem, ds.cnt) for ds in self._dsems if ds.cnt > 0]
        return toks

    def emit(self, eng, fn, deps=(), token=False):
        if self.muted:
            return self.SKIP if token else None
        deps = [d for d in deps if d is not self.SKIP]
        es = self.esem[eng]
        sem, amt = (None, 0)
        tok = None
        if token:
            es.cnt += 1
            sem, amt = es.sem, 1
            tok = (es.sem, es.cnt)
        assert all(d is not None for d in deps), f"None dep on {eng}"
        self.recs[eng].append(Rec(fn, tuple(deps), sem, amt))
        return tok

    def dma(self, eng, dsem, out, in_, deps=(), **kw):
        if self.muted:
            return self.SKIP
        deps = [d for d in deps if d is not self.SKIP]
        dsem.cnt += 16
        tok = (dsem.sem, dsem.cnt)

        def fn(e, out=out, in_=in_, kw=kw):
            return e.dma_start(out=out, in_=in_, **kw)

        assert all(d is not None for d in deps), f"None dep on dma {eng}"
        self.recs[eng].append(Rec(fn, tuple(deps), dsem.sem, 16))
        return tok

    def dmat(self, eng, dsem, out, in_, deps=()):
        if self.muted:
            return self.SKIP
        deps = [d for d in deps if d is not self.SKIP]
        dsem.cnt += 16
        tok = (dsem.sem, dsem.cnt)

        def fn(e, out=out, in_=in_):
            return e.dma_start_transpose(out, in_)

        assert all(d is not None for d in deps), f"None dep on dmat {eng}"
        self.recs[eng].append(Rec(fn, tuple(deps), dsem.sem, 16))
        return tok

    def run(self, block):
        def runner(recs):
            def body(eng):
                waited = {}
                for r in recs:
                    need = {}
                    for (sem, val) in r.deps:
                        sid = id(sem)
                        if waited.get(sid, -1) < val and need.get(sid, (0, 0))[1] < val:
                            need[sid] = (sem, val)
                    for sid, (sem, val) in need.items():
                        eng.wait_ge(sem, val)
                        waited[sid] = val
                    inst = r.fn(eng)
                    if r.sem is not None:
                        assert inst is not None
                        inst.then_inc(r.sem, r.amt)
            return body

        block.tensor(runner(self.recs["pe"]))
        block.scalar(runner(self.recs["act"]))
        block.vector(runner(self.recs["dve"]))
        block.gpsimd(runner(self.recs["pool"]))
        block.sync(runner(self.recs["sp"]))


class Arena:

    def __init__(self, nc, nf32):
        self.cap = None
        self.t = nc.sbuf_tensor("arena", [128, nf32], F32).__enter__()
        self.ap32 = self.t.ap()
        self.ap16 = self.t.ap().bitcast(BF16)
        self.off = 0
        self.nf32 = nf32
        self.marks = {}

    def a32(self, n):
        o = self.off = (self.off + 7) & ~7
        self.off += n
        lim = self.cap if self.cap is not None else self.nf32
        assert self.off <= lim, f"SBUF arena overflow: {self.off} > {lim}"
        return self.ap32[:, o:o + n]

    def a16(self, n):
        n2 = (n + 1) // 2
        o = self.off = (self.off + 7) & ~7
        self.off += n2
        lim = self.cap if self.cap is not None else self.nf32
        assert self.off <= lim, f"SBUF arena overflow: {self.off} > {lim}"
        return self.ap16[:, 2 * o:2 * o + n]

    def mark(self, name):
        self.marks[name] = self.off

    def reset(self, name):
        self.off = self.marks[name]

    def tell(self):
        return self.off

    def at32(self, off, n):
        return self.ap32[:, off:off + n]

    def at16(self, off, n):
        return self.ap16[:, 2 * off:2 * off + n]



def build_program(nc, S, n_cores, stop_after="full", launch="fused"):
    L1 = launch in ("fused", "l1")
    L2 = launch in ("fused", "l2")
    NT = S // 128
    SF = S + 128
    NCH = NT
    SEG4 = [0, 1, 2, 3]

    dram = {}
    in_names = []
    out_names = []

    def din(name, shape, dt=F32):
        t = nc.dram_tensor(name, shape, dt, kind="ExternalInput")
        dram[name] = t.ap()
        in_names.append(name)
        return dram[name]

    def dout(name, shape, dt=F32):
        t = nc.dram_tensor(name, shape, dt, kind="ExternalOutput")
        dram[name] = t.ap()
        out_names.append(name)
        return dram[name]

    def dint(name, shape, dt=F32, addr_space="Local"):
        t = nc.dram_tensor(name, shape, dt, kind="Internal",
                           addr_space=addr_space)
        dram[name] = t.ap()
        return dram[name]

    def d_l1(name, shape, dt=F32):
        return din(name, shape, dt) if L1 else dint(name, shape, mybir.dt.int8)

    def d_l2(name, shape, dt=F32):
        return din(name, shape, dt) if L2 else dint(name, shape, mybir.dt.int8)
    w_in = d_l1("w_in", [NCT, 128, NDB, 128])
    convw = d_l1("convw", [128, 80, 4])
    convb = d_l1("convb", [128, 80])
    nwfm = d_l1("nwfm", [128, 64])
    scr_ln1 = dint("scr_ln1", [SF, D], BF16)
    xin = din("xin", [SF, D])
    dtb = din("dtb", [128, 1])
    alog = din("alog", [128, 1])
    impulse = din("impulse", [128, 1024])
    drow = d_l2("drow", [1, 128])
    selmask = d_l2("selmask", [128, 8])
    w_out = d_l2("w_out", [4, INTER // 128, 128, 1024])
    w1aug = d_l2("w1aug", [4 * SEGT, ZKB, 128, 128])
    w3aug = d_l2("w3aug", [4 * SEGT, ZKB, 128, 128])
    w2aug = d_l2("w2aug", [4, HKB, 128, 1024])
    a13t = d_l2("a13t", [NDB, 128, 4])
    a2t = d_l2("a2t", [4 * SEGT, 128, 2])
    out_hbm = (dout if L2 else dint)("out", [S, D])
    scr_ynt = dint("scr_ynt", [NT, 64, 128, 128], BF16)
    scr_att = dint("scr_att", [S, D], F32)
    scr_x1 = dint("scr_x1", [S, D], F32)
    scr_ln2 = dint("scr_ln2", [S, D], BF16)
    scr_ffn = dint("scr_ffn", [S, D], F32)
    if launch == "fused":
        scr_hbc = dint("scr_hbc", [CONVD, S], BF16)
        scr_sgn = dint("scr_sgn", [INTER, S], BF16)
        scr_stx = dint("scr_stx", [128, INTER + 1], F32)
        scr_stall = dint("scr_stall", [n_cores * 128, INTER + 1], F32)
        dtraw_io = dint("dtraw_io", [128, S + 3], F32)
    elif launch == "l1":
        scr_hbc = dout("scr_hbc", [CONVD, S], BF16)
        scr_sgn = dout("scr_sgn", [INTER, S], BF16)
        scr_stx = dout("scr_stx", [128, INTER + 1], F32)
        scr_stall = dint("scr_stall", [n_cores * 128, INTER + 1], F32)
        dtraw_io = dout("dtraw_io", [128, S + 3], F32)
    else:
        scr_hbc = din("scr_hbc", [CONVD, S], BF16)
        scr_sgn = din("scr_sgn", [INTER, S], BF16)
        scr_stx = dint("scr_stx", [128, INTER + 1], F32)
        scr_stall = din("scr_stall", [n_cores * 128, INTER + 1], F32)
        dtraw_io = din("dtraw_io", [128, S + 3], F32)

    prog = Prog(nc)
    global _LAST_PROG
    _LAST_PROG = prog
    A = Arena(nc, 53180)

    pq = [nc.psum_tensor(f"pq{i}", [128, 1024], F32).__enter__().ap()
          for i in range(2)]
    pg = nc.psum_tensor("pg", [128, 1024], F32).__enter__().ap()
    py6 = nc.psum_tensor("py6", [128, 512], F32).__enter__().ap()
    py7 = nc.psum_tensor("py7", [128, 512], F32).__enter__().ap()
    pqb = [p.bitcast(BF16) for p in pq]
    pb = [pq[0][:, 0:512], pq[0][:, 512:1024], pq[1][:, 0:512],
          pq[1][:, 512:1024], pg[:, 0:512], pg[:, 512:1024], py6, py7]

    ones1 = A.a32(128)
    convw_s = A.a32(320)
    convb_s = A.a32(80)
    dtb_s = A.a32(1)
    alog_s = A.a32(1)
    nwfm_s = A.a32(64)
    imp_s = A.a32(128)
    ident16 = A.a16(128)
    selm_s = A.a32(8)
    A.cap = (A.nf32 - (S + 4)) & ~7
    dt_raw = A.at32(A.cap, S + 3)
    dt_soft = A.a32(S)
    q129 = A.a32(NCH * 129)
    sa_col = A.a32(1)
    stats_a = A.a32(16)
    stats_f = A.a32(16)
    st_m2 = A.a32(1)
    st_r2 = A.a32(1)
    st_b2 = A.a32(1)
    st_q2 = A.a32(1)

    ld_const = prog.dsem("ld_const")
    ld_constg = prog.dsem("ld_constg")

    if L1:
        prog.dma("sp", ld_const, convw_s, convw.rearrange("p a b -> p (a b)"))
        prog.dma("sp", ld_const, convb_s, convb)
        prog.dma("sp", ld_const, nwfm_s, nwfm)
    prog.dma("sp", ld_const, dtb_s, dtb)
    prog.dma("sp", ld_const, alog_s, alog)
    prog.dma("sp", ld_const, imp_s, impulse[:, 0:128])
    if L2:
        prog.dma("sp", ld_const, selm_s, selmask)
    t_consts = ld_const.tok()
    t_c0 = t_c1 = t_c2 = t_c3 = t_c4 = t_c5 = t_c6 = t_consts
    t_ones = prog.emit("dve", lambda e: e.memset(ones1, 1.0), token=True)
    cvals = {"eps": EPS, "mhalf": -0.5, "one": 1.0, "mone": -1.0, "half": 0.5,
             "i8192": 1.0 / INTER}
    cap = {}
    capt = {}
    cap_toks = []
    cap_blk = A.a32(len(cvals))
    for ci, (k, v) in enumerate(cvals.items()):
        capk = cap_blk[:, ci:ci + 1]
        cap[k] = capk
        capt[k] = prog.emit("dve", lambda e, a=capk, v=v: e.memset(a, v),
                            token=True)
        cap_toks.append(capt[k])
    A.mark("phase")
    t_id16 = prog.emit("dve", lambda e: e.tensor_copy(ident16, imp_s), deps=[t_c4],
                       token=True)
    const_toks = [t_c0, t_c1, t_c2, t_c3, t_c4, t_c5, t_c6, t_ones, t_id16] + cap_toks

    prog.muted = not L1
    NTL = SF // 128
    xbuf = [A.a32(D) for _ in range(2)]
    sqdump = A.a16(D)
    st_s = A.a32(1)
    st_q = A.a32(1)
    st_m = A.a32(1)
    st_r = A.a32(1)
    st_b = A.a32(1)
    ln1_stage = [A.a16(D) for _ in range(2)]
    ld_x = [prog.dsem(f"ld_x{i}") for i in range(2)]
    sp_ln1 = [prog.dsem(f"sp_ln1{i}") for i in range(2)]

    slot_free = [None, None]
    stage_free = [None, None]
    prev_apply = None
    for t in range(NTL):
        sl = t % 2
        dep = [slot_free[sl]] if slot_free[sl] else []
        tx = prog.dma("sp", ld_x[sl], xbuf[sl], xin[t * 128:(t + 1) * 128, :],
                      deps=dep)
        pdep = [prev_apply] if prev_apply else []
        tq = prog.emit("act", lambda e, b=xbuf[sl]: e.activation(
            out=sqdump, in_=b, func=AF.Square, accum_out=st_q), deps=[tx] + pdep,
            token=True)
        ts = prog.emit("dve", lambda e, b=xbuf[sl]: e.reduce_sum(
            st_s, b, axis=mybir.AxisListType.X), deps=[tx] + pdep, token=True)
        tm = prog.emit("dve", lambda e: e.tensor_scalar(
            st_m, st_s, 1.0 / D, None, ALU.mult), deps=[ts], token=True)
        tv1 = prog.emit("dve", lambda e: e.tensor_scalar(
            st_q, st_q, 1.0 / D, None, ALU.mult), deps=[tq], token=True)
        tv2 = prog.emit("dve", lambda e: e.tensor_tensor(
            st_b, st_m, st_m, ALU.mult), deps=[tm], token=True)
        tv3 = prog.emit("dve", lambda e: e.tensor_tensor(
            st_q, st_q, st_b, ALU.subtract), deps=[tv1, tv2], token=True)
        tr1 = prog.emit("act", lambda e: e.activation(
            out=st_r, in_=st_q, func=AF.Ln, bias=cap["eps"]),
            deps=[tv3, capt["eps"]], token=True)
        tr2 = prog.emit("act", lambda e: e.activation(
            out=st_r, in_=st_r, func=AF.Exp, scale=cap["mhalf"]),
            deps=[tr1, capt["mhalf"]], token=True)
        tb = prog.emit("dve", lambda e: e.tensor_tensor(
            st_b, st_m, st_r, ALU.mult), deps=[tm, tr2], token=True)
        tb2 = prog.emit("dve", lambda e: e.tensor_scalar(
            st_b, st_b, -1.0, None, ALU.mult), deps=[tb], token=True)
        sdep = [stage_free[sl]] if stage_free[sl] else []
        ta = prog.emit("act", lambda e, b=xbuf[sl], o=ln1_stage[sl]: e.activation(
            out=o, in_=b, func=AF.Identity, scale=st_r, bias=st_b),
            deps=[tr2, tb2, tx] + sdep, token=True)
        slot_free[sl] = ta
        prev_apply = ta
        tsp = prog.dma("sp", sp_ln1[sl], scr_ln1[t * 128:(t + 1) * 128, :],
                       ln1_stage[sl], deps=[ta])
        stage_free[sl] = tsp
    t_ln1_done = [sp_ln1[0].tok(), sp_ln1[1].tok()]

    A.reset("phase")
    bar_a1 = prog.barrier()
    ln1fm = A.a16(NDB * SF)
    ld_fm = prog.dsem("ld_fm")
    for db in range(NDB):
        prog.dmat("act", ld_fm, ln1fm[:, db * SF:(db + 1) * SF],
                  scr_ln1[:, db * 128:(db + 1) * 128], deps=bar_a1)
    t_fm_done = ld_fm.tok()

    def ln1fm_mt(db, c0, c1):
        return ln1fm[:, db * SF + c0:db * SF + c1]

    wtile = [A.a16(NDB * 128) for _ in range(2)]
    convbuf = [A.a16(S + 3) for _ in range(2)]
    acc = [A.a32(S) for _ in range(2)]
    evst = [A.a16(512) for _ in range(4)]
    ld_w = [prog.dsem(f"ld_w{i}") for i in range(2)]
    sp_hbc = [prog.dsem(f"sp_hbc{i}") for i in range(2)]
    sp_sgn = [prog.dsem(f"sp_sgn{i}") for i in range(4)]

    mt_gate = [(128 + i * 512, 128 + (i + 1) * 512) for i in range(4)]
    mt_hbc = [(125 + i * 512, 125 + (i + 1) * 512) for i in range(4)] + [(2173, SF)]
    if S != 2048:
        n4 = (S - 3) // 512
        mt_gate = [(128 + i * 512, min(128 + (i + 1) * 512, SF))
                   for i in range((S + 511) // 512)]
        mt_hbc = [(125 + i * 512, min(125 + (i + 1) * 512, SF))
                  for i in range((S + 3 + 511) // 512)]

    wslot_free = [bar_a1, bar_a1]
    conv_free = [None, None]
    acc_free = [None, None]
    ev_free = [None] * 4
    bank_free = [None] * 8
    last_spill_hbc = None
    for ct in range(NCT):
        is_gate = ct < 64
        is_dt = ct == NCT - 1
        sl = ct % 2
        dep = wslot_free[sl] if isinstance(wslot_free[sl], list) else (
            [wslot_free[sl]] if wslot_free[sl] else [])
        tw = prog.dma("pool", ld_w[sl], wtile[sl],
                      w_in[ct].rearrange("p a b -> p (a b)"), deps=dep)
        mts = mt_gate if is_gate else mt_hbc
        nmt = len(mts)
        mm_toks = []
        bdeps = [bank_free[b] for b in range(nmt) if bank_free[b]]
        for db in range(NDB):
            for mi, (c0, c1) in enumerate(mts):
                lastmm = db == NDB - 1 and mi == nmt - 1
                tk = prog.emit("pe", lambda e, sl=sl, db=db, mi=mi, c0=c0, c1=c1: e.matmul(
                    pb[mi][:, 0:c1 - c0],
                    wtile[sl][:, db * 128:(db + 1) * 128],
                    ln1fm_mt(db, c0, c1),
                    start=(db == 0), stop=(db == NDB - 1)),
                    deps=([tw, t_fm_done] + bdeps) if db == 0 else [],
                    token=lastmm or (db == NDB - 1))
                if db == NDB - 1:
                    mm_toks.append(tk)
        wslot_free[sl] = mm_toks[-1]
        if is_gate:
            for mi, (c0, c1) in enumerate(mts):
                evi = mi % 4
                edep = [ev_free[evi]] if ev_free[evi] else []
                te = prog.emit("act", lambda e, mi=mi, evi=evi, w=c1 - c0: e.activation(
                    out=evst[evi][:, 0:w], in_=pb[mi][:, 0:w], func=AF.Silu),
                    deps=[mm_toks[mi]] + edep, token=True)
                bank_free[mi] = te
                tn = prog.emit("dve", lambda e, evi=evi, ct=ct, w=c1 - c0: e.tensor_scalar(
                    evst[evi][:, 0:w], evst[evi][:, 0:w],
                    nwfm_s[:, ct:ct + 1], None, ALU.mult),
                    deps=[te, t_c5], token=True)
                tsp = prog.dma("sp", sp_sgn[evi],
                               scr_sgn[ct * 128:(ct + 1) * 128, c0 - 128:c1 - 128],
                               evst[evi][:, 0:c1 - c0], deps=[tn])
                ev_free[evi] = tsp
        elif not is_dt:
            cdep = [conv_free[sl]] if conv_free[sl] else []
            ev_toks = []
            for mi, (c0, c1) in enumerate(mts):
                te = prog.emit("act", lambda e, sl=sl, mi=mi, c0=c0, c1=c1: e.activation(
                    out=convbuf[sl][:, c0 - 125:c1 - 125], in_=pb[mi][:, 0:c1 - c0],
                    func=AF.Copy), deps=[mm_toks[mi]] + cdep, token=True)
                bank_free[mi] = te
                ev_toks.append(te)
            adep = [acc_free[sl]] if acc_free[sl] else []
            cti = ct - 64
            tt = prog.emit("dve", lambda e, sl=sl, cti=cti: e.tensor_scalar(
                acc[sl], convbuf[sl][:, 0:S], convw_s[:, cti * 4:cti * 4 + 1],
                None, ALU.mult), deps=ev_toks + adep + [t_c0], token=True)
            for k in (1, 2, 3):
                eng = "dve"
                tt = prog.emit(eng, lambda e, sl=sl, cti=cti, k=k: e.scalar_tensor_tensor(
                    acc[sl], convbuf[sl][:, k:S + k], convw_s[:, cti * 4 + k:cti * 4 + k + 1],
                    acc[sl], ALU.mult, ALU.add), deps=[tt], token=True)
            conv_free[sl] = tt
            tsl = prog.emit("act", lambda e, sl=sl, cti=cti: e.activation(
                out=convbuf[sl][:, 0:S], in_=acc[sl], func=AF.Silu,
                bias=convb_s[:, cti:cti + 1]),
                deps=[tt, t_c1], token=True)
            tsp = prog.dma("sp", sp_hbc[sl], scr_hbc[cti * 128:(cti + 1) * 128, :],
                           convbuf[sl][:, 0:S], deps=[tsl])
            acc_free[sl] = tsl
            conv_free[sl] = tsp
            last_spill_hbc = tsp
        else:
            for mi, (c0, c1) in enumerate(mts):
                te = prog.emit("act", lambda e, mi=mi, c0=c0, c1=c1: e.activation(
                    out=dt_raw[:, c0 - 125:c1 - 125], in_=pb[mi][:, 0:c1 - c0],
                    func=AF.Copy), deps=[mm_toks[mi]], token=True)
                bank_free[mi] = te
            t_dtraw = te

    t_hbc_done = [sp_hbc[0].tok(), sp_hbc[1].tok()]
    t_sgn_done = [q.tok() for q in sp_sgn]

    dbg = {}
    if stop_after == "A":
        finish(prog, t_hbc_done + t_sgn_done + [t_dtraw])
        return {"dram": dram, "dbg": dbg, "inputs": in_names, "outputs": out_names}

    prog.muted = False
    bar_a3 = prog.barrier()
    A.reset("phase")
    ld_dtr = prog.dsem("ld_dtr")
    if launch == "l2":
        t_dtraw = prog.dma("sp", ld_dtr, dt_raw, dtraw_io)
    dtv = dt_raw[:, 3:3 + S]
    t_ab = prog.emit("act", lambda e: e.activation(
        out=dt_soft, in_=dtv, func=AF.Abs, bias=dtb_s), deps=[t_dtraw, t_consts],
        token=True)
    t_ex = prog.emit("act", lambda e: e.activation(
        out=dt_soft, in_=dt_soft, func=AF.Exp, scale=cap["mone"]),
        deps=[t_ab, capt["mone"]], token=True)
    t_ln = prog.emit("act", lambda e: e.activation(
        out=dt_soft, in_=dt_soft, func=AF.Ln, bias=cap["one"]),
        deps=[t_ex, capt["one"]], token=True)
    A.mark("preB")
    relu_t = A.a32(S)
    t_rl = prog.emit("act", lambda e: e.activation(
        out=relu_t, in_=dtv, func=AF.Relu, bias=dtb_s),
        deps=[t_dtraw, t_consts] + bar_a3, token=True)
    t_dts = prog.emit("dve", lambda e: e.tensor_add(dt_soft, dt_soft, relu_t),
                      deps=[t_ln, t_rl], token=True)
    aneg = A.a32(1)
    t_an = prog.emit("act", lambda e: e.activation(
        out=aneg, in_=alog_s, func=AF.Exp), deps=[t_consts], token=True)
    t_an2 = prog.emit("dve", lambda e: e.tensor_scalar(
        aneg, aneg, -1.0, None, ALU.mult), deps=[t_an], token=True)
    t_adt = prog.emit("dve", lambda e: e.tensor_scalar(
        relu_t, dt_soft, aneg, None, ALU.mult), deps=[t_dts, t_an2], token=True)
    q129r = q129.rearrange("p (c l) -> p c l", l=129)
    t_qpad = prog.emit("dve", lambda e: e.memset(q129r[:, :, 128:129], 1.0),
                       token=True)
    t_q = prog.emit("act", lambda e: e.activation(
        out=q129r[:, :, 0:128],
        in_=relu_t.rearrange("p (c l) -> p c l", l=128), func=AF.Exp),
        deps=[t_adt, t_qpad], token=True)
    if launch != "l2":
        t_sa = prog.emit("dve", lambda e: e.reduce_sum(
            sa_col, relu_t, axis=mybir.AxisListType.X), deps=[t_adt],
            token=True)
    else:
        t_sa = Prog.SKIP

    A.reset("preB")
    S_off = A.tell()
    S_st = A.a32(INTER)
    xb_tok = [A.a16(9216) for _ in range(2)]
    xdt_off = A.tell()
    x_dt = A.a16(INTER)
    xE = A.a16(INTER)
    drow_s = A.a32(128)
    drow_lh = A.a16(128)
    c_fm = [A.a32(GN)]
    b_fm = [A.a32(GN)]
    gmt = A.a16(GN)
    lb_off = A.tell()
    Lb = [A.a16(1024) for _ in range(2)]
    MTb = [A.a16(1024) for _ in range(3)]
    sgn_off = A.tell()
    sgn_ch = A.a16(INTER // 2)
    Yfull = A.a16(INTER)
    ynev = [A.a16(1024), A.a16(1024)]
    sq16 = A.a32(16)
    rstd_b = A.a32(1)
    stg1_off = A.tell()
    stg1 = [A.a32(512), A.a32(512)]
    stg2 = [A.a32(512), A.a32(512)]
    stg3 = [A.a16(512), A.a16(512)]
    dtT = A.a32(128)
    e2T = A.a32(128)
    fT = A.a32(128)
    e2b = A.a32(128)
    fb = A.a32(128)
    dtot = A.a32(1)
    dtrow = A.a32(128)
    Dsb = A.a32(128)
    rowq_reg = A.a32(1024)
    rowq = [rowq_reg[0:1, :], rowq_reg[32:33, :]]
    imp8b = A.a16(1024)
    qbc = [A.a32(1024), A.a32(1024)]
    wts8 = A.at32(stg1_off, 128)
    wv8 = A.a32(8)
    sa8 = A.a32(8)
    zrow = A.a32(4)

    qscr = dint("qscr", [NCH, 128, 128], F32)
    wscr = dint("wscr", [8, 128], F32)

    ld_xb = [prog.dsem(f"ld_xb{i}") for i in range(2)]
    ld_cb = [prog.dsem(f"ld_cb{i}") for i in range(2)]
    ld_sgn = prog.dsem("ld_sgn")
    ld_row = [prog.dsem("ld_row0"), prog.dsem("ld_row1")]
    ld_sa = prog.dsem("ld_sa")
    ld_rst = [prog.dsem(f"ld_rst{i}") for i in range(2)]
    sp_q = prog.dsem("sp_q")
    sp_ynt = prog.dsem("sp_ynt")
    sp_ynt2 = prog.dsem("sp_ynt2")
    sp_st = prog.dsem("sp_st")
    cc_sem = prog.dsem("cc_sem")
    ld_dbc = prog.dsem("ld_dbc")

    ld_imp8 = prog.dsem("ld_imp8")
    if L2:
        t_dr0 = prog.dma("pool", ld_dbc, drow_s[0:1, :], dram["drow"],
                         deps=bar_a3)
        t_imp8 = prog.dma("pool", ld_imp8, imp8b, impulse, deps=bar_a3)
    else:
        t_dr0 = t_imp8 = Prog.SKIP
    t_szero = prog.emit("dve", lambda e: e.memset(S_st, 0.0),
                        deps=bar_a3 + [t_q, t_sa, t_adt], token=True)
    t_zr = prog.emit("dve", lambda e: e.memset(zrow, 0.0), deps=bar_a3,
                     token=True)

    frkeys = ("ynev0", "ynev1",
              "py6", "py7", "pg", "pgh0", "pgh1", "pq0", "pq1",
              "dtT", "e2T", "fT", "e2b", "fb", "dtot", "dtrow", "Dsb",
              "rowq0", "rowq1", "gmt", "Lb0", "Lb1",
              "MT0", "MT1", "MT2", "stg10", "stg11", "stg20", "stg21",
              "stg30", "stg31", "xdt", "xE", "Yfull", "ynT", "sgn")
    fr = {k: [t_szero] for k in frkeys}

    if L2:
        t_dmm = prog.emit("pe", lambda e: e.matmul(
            py6[:, 0:128], ones1[0:1, 0:128], drow_s[0:1, :],
            start=True, stop=True), deps=[t_dr0, t_ones] + fr["py6"],
            token=True)
        t_dlh = prog.emit("dve", lambda e: e.tensor_copy(
            drow_lh, py6[:, 0:128]), deps=[t_dmm], token=True)
        fr["py6"] = [t_dlh]
    else:
        t_dlh = Prog.SKIP

    def qch(c):
        return q129[:, c * 129:(c + 1) * 129]

    def chunk_scans(c, want_f):
        out = {}
        t_e2 = prog.emit("dve", lambda e, c=c: e.tensor_tensor_scan(
            e2b[:, 127::-1], qch(c)[:, 128:0:-1],
            zrow[:, 0:1].broadcast_to([128, 128]), 1.0, ALU.mult, ALU.add),
            deps=[t_q, t_zr] + fr["e2b"], token=True)
        out["e2"] = t_e2
        fr["e2b"] = [t_e2]
        items = [("dtT", dt_soft[:, c * 128:(c + 1) * 128], dtT,
                  [t_dts] + fr["dtT"]),
                 ("e2T", e2b, e2T, [t_e2] + fr["e2T"])]
        if want_f:
            t_f = prog.emit("dve", lambda e, c=c: e.tensor_tensor_scan(
                fb, qch(c)[:, 0:128], zrow[:, 0:1].broadcast_to([128, 128]),
                1.0, ALU.mult, ALU.add), deps=[t_q, t_zr] + fr["fb"], token=True)
            out["f"] = t_f
            fr["fb"] = [t_f]
            items.append(("fT", fb, fT, [t_f] + fr["fT"]))
        for name, src, dst, dd in items:
            tt = prog.emit("pe", lambda e, src=src: e.transpose(
                py6[:, 0:128], src, imp_s), deps=dd + fr["py6"] + [t_consts],
                token=True)
            te = prog.emit("dve", lambda e, dst=dst: e.tensor_copy(
                dst, py6[:, 0:128]), deps=[tt] + fr[name], token=True)
            fr["py6"] = [te]
            fr[name] = []
            out[name] = te
        t_dc = prog.emit("dve", lambda e, c=c: e.tensor_tensor(
            dtot, e2b[:, 0:1], qch(c)[:, 0:1], ALU.mult),
            deps=[t_e2, t_q] + fr["dtot"], token=True)
        tt = prog.emit("pe", lambda e: e.transpose(
            py6[0:1, 0:128], dtot, imp_s), deps=[t_dc] + fr["py6"], token=True)
        fr["dtot"] = [tt]
        te = prog.emit("dve", lambda e: e.tensor_copy(
            dtrow[0:1, :], py6[0:1, 0:128]), deps=[tt] + fr["dtrow"],
            token=True)
        fr["py6"] = [te]
        fr["dtrow"] = []
        out["dtrow"] = te
        return out

    def emit_dbig(t_dtrow):
        t_db = prog.emit("pe", lambda e: e.matmul(
            py6[:, 0:128], ones1[0:1, 0:128], dtrow[0:1, 0:128],
            start=True, stop=True), deps=[t_dtrow, t_ones] + fr["py6"],
            token=True)
        fr["dtrow"] = [t_db]
        t_dsb = prog.emit("dve", lambda e: e.tensor_copy(Dsb, py6[:, 0:128]),
                          deps=[t_db] + fr["Dsb"], token=True)
        fr["py6"] = [t_dsb]
        fr["Dsb"] = []
        return t_dsb

    def emit_states_g8(c, g8, t_xE, t_dsb, yoff_dep, supd_prev, u1_eng="pool"):
        g = g8 // 2
        pqi = g8 % 2
        t_last = None
        for j8 in range(8):
            h = g8 * 8 + j8
            t_last = prog.emit("pe", lambda e, c=c, g=g, j8=j8, h=h, pqi=pqi:
                               e.matmul(
                pq[pqi][:, j8 * 64:(j8 + 1) * 64],
                xb_tok[c % 2][:, INTER + g * 128:INTER + (g + 1) * 128],
                xE[:, h * 64:(h + 1) * 64], start=True, stop=True),
                deps=([t_xE] + fr[f"pq{pqi}"] + yoff_dep) if j8 == 0 else [],
                token=(j8 == 7))
        dep_prev = [supd_prev[g8]] if supd_prev else []
        t_u1 = prog.emit(u1_eng, lambda e, g8=g8: e.tensor_tensor(
            S_st[:, g8 * 512:(g8 + 1) * 512],
            S_st[:, g8 * 512:(g8 + 1) * 512],
            Dsb[:, g8 * 8:(g8 + 1) * 8].unsqueeze(2)
                .broadcast_to([128, 8, 64]), ALU.mult),
            deps=[t_dsb] + dep_prev + yoff_dep, token=True)
        t_u2 = prog.emit("dve", lambda e, g8=g8, pqi=pqi: e.tensor_tensor(
            S_st[:, g8 * 512:(g8 + 1) * 512],
            S_st[:, g8 * 512:(g8 + 1) * 512],
            pq[pqi][:, 0:512], ALU.add), deps=[t_last, t_u1], token=True)
        fr[f"pq{pqi}"] = [t_u2]
        return t_u2

    sp_qs = prog.dsem("sp_qs")
    t_qspill = prog.dma("sp", sp_qs, qscr.rearrange("c p l -> p c l"),
                        q129r[:, :, 0:128], deps=[t_q])

    if launch == "l2":
        t_hbc_done = []
        t_sgn_done = []
    prog.muted = launch == "l2"
    xb_free = [bar_a3, bar_a3]
    supd_prev = None
    for c in range(NCH):
        sl = c % 2
        txb = prog.dmat("act", ld_xb[sl], xb_tok[sl][:, 0:9216],
                        scr_hbc[0:9216, c * 128:(c + 1) * 128],
                        deps=list(xb_free[sl] if isinstance(xb_free[sl], list)
                                  else [xb_free[sl]]) + t_hbc_done)
        trr = chunk_scans(c, want_f=False)
        t_de = prog.emit("dve", lambda e: e.tensor_tensor(
            dtT, dtT, e2T, ALU.mult), deps=[trr["dtT"], trr["e2T"]], token=True)
        fr["e2T"] = [t_de]
        t_xE = prog.emit("dve", lambda e, sl=sl: e.tensor_tensor(
            xE, xb_tok[sl][:, 0:INTER].rearrange("s (h p) -> s h p", h=H),
            dtT.unsqueeze(2).broadcast_to([128, 128, 64]), ALU.mult),
            deps=[t_de, txb] + fr["xE"], token=True)
        fr["dtT"] = [t_xE]
        t_dsb = emit_dbig(trr["dtrow"])
        supd = []
        for g8 in range(16):
            supd.append(emit_states_g8(c, g8, t_xE, t_dsb, [], supd_prev,
                                       u1_eng="dve"))
        supd_prev = supd
        fr["Dsb"] = [supd[-1]]
        fr["xE"] = [supd[-1]]
        xb_free[sl] = [supd[-1]]
    t_spr = prog.dma("sp", sp_st, scr_stx[:, 0:INTER], S_st, deps=supd_prev)
    t_sps = prog.dma("sp", sp_st, scr_stx[:, INTER:INTER + 1], sa_col,
                     deps=[t_sa], allow_slow_non_contiguous=True)
    t_spst = sp_st.tok() if launch != "l2" else Prog.SKIP
    if launch == "l1":
        t_dtsp = prog.dma("sp", sp_st, dtraw_io, dt_raw, deps=[t_dtraw])
        finish(prog, [sp_st.tok(), sp_hbc[0].tok(), sp_hbc[1].tok()]
               + [q.tok() for q in sp_sgn])
        dbg["S_st_off"] = S_off
        return {"dram": dram, "dbg": dbg, "inputs": in_names,
                "outputs": out_names}
    if launch == "fused":
        rec = Rec(lambda e: e.collective_compute(
            "AllGather", ALU.bypass, replica_groups=[list(range(n_cores))],
            ins=[scr_stx[:, :]], outs=[scr_stall[:, :]]),
            (t_spst,), cc_sem.sem, 16)
        prog.recs["pool"].append(rec)
        cc_sem.cnt += 16
        t_cc = cc_sem.tok()
    else:
        prog.muted = False
        t_cc = Prog.SKIP

    t_sa8 = prog.dma("sp", ld_sa, sa8[:, 0:n_cores],
                     scr_stall[:, INTER:INTER + 1].rearrange(
                         "(j h) one -> h (j one)", h=128),
                     deps=[t_cc], allow_slow_non_contiguous=True)
    t_sam = prog.emit("dve", lambda e: e.tensor_tensor(
        sa8[:, 0:n_cores], sa8[:, 0:n_cores], selm_s[:, 0:n_cores], ALU.mult),
        deps=[t_sa8, t_consts], token=True)
    if n_cores > 1:
        t_rc = prog.emit("dve", lambda e: e.tensor_tensor_scan(
            wv8[:, n_cores - 1::-1], sa8[:, n_cores - 1::-1],
            zrow[:, 0:1].broadcast_to([128, n_cores]), 0.0, ALU.add, ALU.add),
            deps=[t_sam, t_zr], token=True)
        t_exc = prog.emit("dve", lambda e: e.tensor_tensor(
            wv8[:, 0:n_cores], wv8[:, 0:n_cores], sa8[:, 0:n_cores],
            ALU.subtract), deps=[t_rc], token=True)
    else:
        t_exc = prog.emit("dve", lambda e: e.memset(wv8[:, 0:1], 0.0),
                          deps=[t_sam], token=True)
    t_wexp = prog.emit("act", lambda e: e.activation(
        out=wv8[:, 0:n_cores], in_=wv8[:, 0:n_cores], func=AF.Exp),
        deps=[t_exc], token=True)
    t_wm = prog.emit("dve", lambda e: e.tensor_tensor(
        wv8[:, 0:n_cores], wv8[:, 0:n_cores], selm_s[:, 0:n_cores], ALU.mult),
        deps=[t_wexp], token=True)
    t_wt = prog.emit("pe", lambda e: e.transpose(
        py6[0:n_cores, 0:128], wv8[:, 0:n_cores], imp_s),
        deps=[t_wm] + fr["py6"], token=True)
    t_we = prog.emit("dve", lambda e: e.tensor_copy(
        wts8[0:n_cores, :], py6[0:n_cores, 0:128]), deps=[t_wt], token=True)
    fr["py6"] = [t_we]
    t_wsp = prog.dma("sp", sp_st, wscr[0:n_cores, :], wts8[0:n_cores, :],
                     deps=[t_we])
    fr["stg10"] = [t_wsp]
    t_wrow = prog.dma("act", ld_row[0], rowq[0][:, 0:n_cores * 128],
                      wscr.rearrange("a b -> (a b)")[0:n_cores * 128]
                      .unsqueeze(0), deps=[t_wsp])
    wb_toks = []
    nwb = (n_cores * 128 + 511) // 512
    for half in range(nwb):
        w0, w1 = half * 512, min((half + 1) * 512, n_cores * 128)
        twb = prog.emit("pe", lambda e, w0=w0, w1=w1: e.matmul(
            pg[:, w0:w1], ones1[0:1, 0:128],
            rowq[0][:, w0:w1], start=True, stop=True),
            deps=[t_wrow, t_ones] + (supd_prev[:1] if supd_prev else []) +
            fr["pg"], token=True)
        wb_toks.append(twb)
    fr["rowq0"] = [wb_toks[-1]]
    rstage = [A.at32(sgn_off, 2048), A.at32(sgn_off + 2048, 2048)]
    t_sz2 = prog.emit("dve", lambda e: e.memset(S_st, 0.0),
                      deps=(supd_prev or []) + [t_spst] + bar_a3, token=True)
    rst_free = [[t_sz2], [t_sz2]]
    t_acc = t_sz2
    for j in range(n_cores):
        for hf in range(4):
            sl = (j * 4 + hf) % 2
            trj = prog.dma("sp", ld_rst[sl], rstage[sl],
                           scr_stall[j * 128:(j + 1) * 128,
                                     hf * 2048:(hf + 1) * 2048],
                           deps=rst_free[sl] + [t_cc])
            h0 = j * 128 + hf * 32
            t_m = prog.emit("dve", lambda e, sl=sl, h0=h0: e.tensor_tensor(
                rstage[sl], rstage[sl],
                pg[:, h0:h0 + 32].unsqueeze(2)
                    .broadcast_to([128, 32, 64]), ALU.mult),
                deps=[trj] + wb_toks, token=True)
            t_acc = prog.emit("dve", lambda e, sl=sl, hf=hf: e.tensor_tensor(
                S_st[:, hf * 2048:(hf + 1) * 2048],
                S_st[:, hf * 2048:(hf + 1) * 2048],
                rstage[sl], ALU.add), deps=[t_m, t_acc], token=True)
            rst_free[sl] = [t_acc]
    t_sinit = t_acc
    fr["pg"] = [t_sinit]
    fr["sgn"] = [t_sinit]

    dbg["S_st_off"] = S_off
    if stop_after == "Bpre":
        finish(prog, [t_sinit])
        return {"dram": dram, "dbg": dbg, "inputs": in_names, "outputs": out_names}

    if launch == "l2":
        xb_free = [bar_a3, bar_a3]
    supd_prev = [t_sinit] * 16
    PIPE = 2
    for c in range(NCH):
        sl = c % 2
        txb = prog.dmat("act", ld_xb[sl], xb_tok[sl][:, 0:9216],
                        scr_hbc[0:9216, c * 128:(c + 1) * 128],
                        deps=xb_free[sl])
        tsg0 = prog.dmat("act", ld_sgn, sgn_ch,
                         scr_sgn[0:4096, c * 128:(c + 1) * 128],
                         deps=fr["sgn"])
        tbf = prog.dma("pool", ld_cb[0],
                       b_fm[0].rearrange("n (g l) -> n g l", g=G),
                       scr_hbc[INTER:INTER + GN, c * 128:(c + 1) * 128]
                       .rearrange("(g n) l -> n g l", g=G),
                       deps=fr["gmt"])
        tcf = prog.dma("pool", ld_cb[1],
                       c_fm[0].rearrange("n (g l) -> n g l", g=G),
                       scr_hbc[INTER + GN:CONVD, c * 128:(c + 1) * 128]
                       .rearrange("(g n) l -> n g l", g=G),
                       deps=fr["gmt"])
        trr = chunk_scans(c, want_f=True)
        t_dsb = emit_dbig(trr["dtrow"])
        t_xdt = prog.emit("pool", lambda e, sl=sl: e.tensor_tensor(
            x_dt.rearrange("s (h p) -> s h p", h=H),
            xb_tok[sl][:, 0:INTER].rearrange("s (h p) -> s h p", h=H),
            dtT.unsqueeze(2).broadcast_to([128, 128, 64]), ALU.mult),
            deps=[trr["dtT"], txb] + fr["xdt"], token=True)
        fr["dtT"] = [t_xdt]
        t_xE = prog.emit("pool", lambda e: e.tensor_tensor(
            xE.rearrange("s (h p) -> s h p", h=H),
            x_dt.rearrange("s (h p) -> s h p", h=H),
            e2T.unsqueeze(2).broadcast_to([128, 128, 64]), ALU.mult),
            deps=[t_xdt, trr["e2T"]] + fr["xE"], token=True)
        fr["e2T"] = [t_xE]
        t_g = None
        for g in range(G):
            t_g = prog.emit("pe", lambda e, g=g: e.matmul(
                pg[:, g * 128:(g + 1) * 128],
                b_fm[0][:, g * 128:(g + 1) * 128],
                c_fm[0][:, g * 128:(g + 1) * 128], start=True, stop=True),
                deps=([tbf, tcf] + fr["pg"]) if g == 0 else [],
                token=(g == G - 1))
        t_gme = prog.emit("dve", lambda e: e.tensor_copy(gmt, pg[:, 0:1024]),
                          deps=[t_g] + fr["gmt"], token=True)
        fr["pg"] = [t_gme]
        fr["pgh0"] = [t_gme]
        fr["pgh1"] = [t_gme]

        supd = [None] * 16
        sq_toks = []
        t3_toks = []
        scan_toks = [None] * 16
        mt_toks = [None] * 16
        tyd_last = None
        for i in range(16 + PIPE):
            if i < 16:
                g8 = i
                g = g8 // 2
                pqi = g8 % 2
                r4 = g8 % 3
                r2 = g8 % 2
                trow = prog.dma("sp", ld_row[r2], qbc[r2],
                                qscr[c, g8 * 8:(g8 + 1) * 8, :]
                                .rearrange("a b -> (a b)").unsqueeze(0)
                                .broadcast_to([128, 1024]),
                                deps=fr[f"rowq{r2}"] + [t_qspill])
                rowtok = prog.emit("act", lambda e, r2=r2: e.memzero(
                    qbc[r2].rearrange("s (h l) -> s h l", l=128)[:, :, 0:1]),
                    deps=[trow], token=True)
                tsc = prog.emit("dve", lambda e, r2=r2:
                                e.tensor_tensor_scan(
                    Lb[r2], qbc[r2], imp8b,
                    0.0, ALU.mult, ALU.add),
                    deps=[rowtok, t_imp8] + fr[f"Lb{r2}"], token=True)
                fr[f"rowq{r2}"] = [tsc]
                scan_toks[g8] = tsc
                tmt = prog.emit("pool", lambda e, g=g, r2=r2, r4=r4:
                                e.tensor_tensor(
                    MTb[r4].rearrange("s (j l) -> s j l", j=8),
                    Lb[r2].rearrange("s (j l) -> s j l", j=8),
                    gmt[:, g * 128:(g + 1) * 128].unsqueeze(1)
                    .broadcast_to([128, 8, 128]), ALU.mult),
                    deps=[tsc, t_gme] + fr[f"MT{r4}"], token=True)
                mt_toks[g8] = tmt
                fr[f"Lb{r2}"] = [tmt]
            if i >= PIPE:
                g8 = i - PIPE
                g = g8 // 2
                pqi = g8 % 2
                r4 = g8 % 3
                par = g8 % 2
                pgh = pg[:, par * 512:(par + 1) * 512]
                pyo = py6 if par == 0 else py7
                pyname = "py6" if par == 0 else "py7"
                tyd = None
                for j8 in range(8):
                    h = g8 * 8 + j8
                    tyd = prog.emit("pe", lambda e, j8=j8, h=h, r4=r4, pgh=pgh:
                                    e.matmul(
                        pgh[:, j8 * 64:(j8 + 1) * 64],
                        MTb[r4][:, j8 * 128:(j8 + 1) * 128],
                        x_dt[:, h * 64:(h + 1) * 64], start=True, stop=True),
                        deps=([mt_toks[g8], t_xdt] + fr[f"pgh{par}"])
                        if j8 == 0 else [], token=(j8 == 7))
                tyd_last = tyd
                fr[f"MT{r4}"] = [tyd]
                tyo = None
                for j8 in range(8):
                    h = g8 * 8 + j8
                    tyo = prog.emit("pe", lambda e, j8=j8, h=h, g=g, pyo=pyo:
                                    e.matmul(
                        pyo[:, j8 * 64:(j8 + 1) * 64],
                        c_fm[0][:, g * 128:(g + 1) * 128],
                        S_st[:, h * 64:(h + 1) * 64], start=True, stop=True),
                        deps=([tcf, supd_prev[g8]] + fr[pyname])
                        if j8 == 0 else [], token=(j8 == 7))
                s2i = g8 % 2
                t1 = prog.emit("dve", lambda e, g8=g8, pyo=pyo, s2i=s2i:
                               e.tensor_tensor(
                    stg1[s2i].rearrange("s (j p) -> s j p", j=8),
                    pyo.rearrange("s (j p) -> s j p", j=8),
                    fT[:, g8 * 8:(g8 + 1) * 8].unsqueeze(2)
                    .broadcast_to([128, 8, 64]), ALU.mult),
                    deps=[tyo, trr["fT"]] + fr[f"stg1{s2i}"], token=True)
                fr[pyname] = [t1]
                t2 = prog.emit("dve", lambda e, pgh=pgh, s2i=s2i:
                               e.tensor_tensor(
                    stg2[s2i], stg1[s2i], pgh, ALU.add),
                    deps=[t1, tyd] + fr[f"stg2{s2i}"], token=True)
                fr[f"stg1{s2i}"] = [t2]
                fr[f"pgh{par}"] = [t2]
                dxs = prog.emit("dve", lambda e, sl=sl, g8=g8, s2i=s2i:
                                e.tensor_tensor(
                    stg3[s2i].rearrange("s (j p) -> s j p", j=8),
                    xb_tok[sl][:, g8 * 512:(g8 + 1) * 512]
                    .rearrange("s (j p) -> s j p", j=8),
                    drow_lh[:, g8 * 8:(g8 + 1) * 8].unsqueeze(2)
                    .broadcast_to([128, 8, 64]), ALU.mult),
                    deps=[txb, t_dlh] + fr[f"stg3{s2i}"], token=True)
                t3 = prog.emit("pool", lambda e, g8=g8, s2i=s2i:
                               e.tensor_tensor(
                    Yfull[:, g8 * 512:(g8 + 1) * 512], stg2[s2i], stg3[s2i],
                    ALU.add), deps=[t2, dxs] + fr["Yfull"], token=True)
                t3_toks.append(t3)
                fr[f"stg2{s2i}"] = [t3]
                tsq = prog.emit("act", lambda e, g8=g8, s2i=s2i: e.activation(
                    out=stg3[s2i], in_=Yfull[:, g8 * 512:(g8 + 1) * 512],
                    func=AF.Square, accum_out=sq16[:, g8:g8 + 1]),
                    deps=[t3], token=True)
                fr[f"stg3{s2i}"] = [tsq]
                sq_toks.append(tsq)
                supd[g8] = emit_states_g8(c, g8, t_xE, t_dsb, [tyo], supd_prev)
        supd_prev = supd
        fr["Dsb"] = [supd[-1]]
        fr["xE"] = [supd[-1]]
        fr["xdt"] = [tyd_last, supd[-1]]
        xb_free[sl] = [supd[-1], t3_toks[-1]]
        fr["gmt"] = [mt_toks[15]]
        tss = prog.emit("dve", lambda e: e.reduce_sum(
            rstd_b, sq16[:, 0:16], axis=mybir.AxisListType.X),
            deps=sq_toks, token=True)
        tr1 = prog.emit("act", lambda e: e.activation(
            out=rstd_b, in_=rstd_b, func=AF.Ln, scale=cap["i8192"],
            bias=cap["eps"]), deps=[tss, capt["i8192"], capt["eps"]],
            token=True)
        tr2 = prog.emit("act", lambda e: e.activation(
            out=rstd_b, in_=rstd_b, func=AF.Exp, scale=cap["mhalf"]),
            deps=[tr1, capt["mhalf"]], token=True)
        yn1b = A.at16(xdt_off, INTER)
        yn1 = prog.emit("dve", lambda e, yn1b=yn1b: e.tensor_scalar(
            yn1b, Yfull, rstd_b, None, ALU.mult),
            deps=[tr2] + t3_toks + sq_toks + fr["xdt"] + [supd[-1]],
            token=True)
        yn2a = prog.emit("pool", lambda e, yn1b=yn1b: e.tensor_tensor(
            Yfull[:, 0:4096], yn1b[:, 0:4096], sgn_ch, ALU.mult),
            deps=[yn1, tsg0], token=True)
        tsg1 = prog.dmat("act", ld_sgn, sgn_ch,
                         scr_sgn[4096:INTER, c * 128:(c + 1) * 128],
                         deps=[yn2a])
        yn2 = prog.emit("pool", lambda e, yn1b=yn1b: e.tensor_tensor(
            Yfull[:, 4096:INTER], yn1b[:, 4096:INTER], sgn_ch, ALU.mult),
            deps=[yn1, tsg1, yn2a], token=True)
        fr["sgn"] = [yn2]
        fr["xdt"] = [yn2]
        for grp in range(8):
            gpq = grp % 2
            tt = None
            for k in range(8):
                cb = grp * 8 + k
                tt = prog.emit("pe", lambda e, cb=cb, k=k, gpq=gpq: e.transpose(
                    pqb[gpq][:, k * 128:(k + 1) * 128],
                    Yfull[:, cb * 128:(cb + 1) * 128], ident16),
                    deps=([yn2a if grp < 4 else yn2, t_id16] +
                          fr[f"pq{gpq}"]) if k == 0 else [],
                    token=(k == 7))
            tev = prog.emit("dve", lambda e, gpq=gpq: e.tensor_copy(
                ynev[gpq], pqb[gpq][:, 0:1024]),
                deps=[tt] + fr[f"ynev{gpq}"], token=True)
            fr[f"pq{gpq}"] = [tev]
            tspy = prog.dma("sp" if gpq == 0 else "act",
                            sp_ynt if gpq == 0 else sp_ynt2,
                            scr_ynt[c, grp * 8:(grp + 1) * 8]
                            .rearrange("cb c s -> c cb s"),
                            ynev[gpq].rearrange("c (cb s) -> c cb s", cb=8),
                            deps=[tev])
            fr[f"ynev{gpq}"] = [tspy]
        fr["Yfull"] = [tev]
    t_yn_done = [sp_ynt.tok(), sp_ynt2.tok()]

    dbg["S_st_off"] = S_off
    dbg["arena_used"] = A.off
    if stop_after == "B":
        finish(prog, t_yn_done + [supd_prev[-1]])
        return {"dram": dram, "dbg": dbg, "inputs": in_names, "outputs": out_names}

    bar_b = prog.barrier()
    A.reset("preB")
    wout_off = A.tell()
    wout_q = A.a16(64 * 1024)
    ynt_st = [A.a16(INTER) for _ in range(2)]
    cstage = [A.a32(1024) for _ in range(2)]
    cdump = A.a16(4096)
    stat_s = A.a32(64)
    stat_q = A.a32(64)
    xstat_s = A.a32(16)
    xstat_q = A.a32(16)
    A.mark("C")
    ld_wo = prog.dsem("ld_wo")
    ld_yt = [prog.dsem(f"ld_yt{i}") for i in range(2)]
    sp_att = [prog.dsem(f"sp_att{i}") for i in range(2)]
    sp_x1 = [prog.dsem(f"sp_x1{i}") for i in range(2)]
    sp_ln2 = [prog.dsem(f"sp_ln2{i}") for i in range(2)]
    ld_ep = [prog.dsem(f"ld_ep{i}") for i in range(4)]
    sp_out = [prog.dsem("sp_out0"), prog.dsem("sp_out1")]

    coef_free = [None]

    def emit_ln_coef(sum_ap, sq_ap, deps):
        cd = deps + (coef_free[0] or [])
        tm = prog.emit("dve", lambda e: e.tensor_scalar(
            st_m2, sum_ap, 1.0 / D, None, ALU.mult), deps=cd, token=True)
        tv1 = prog.emit("dve", lambda e: e.tensor_scalar(
            st_q2, sq_ap, 1.0 / D, None, ALU.mult), deps=cd, token=True)
        tv2 = prog.emit("dve", lambda e: e.tensor_tensor(
            st_b2, st_m2, st_m2, ALU.mult), deps=[tm], token=True)
        tv3 = prog.emit("dve", lambda e: e.tensor_tensor(
            st_q2, st_q2, st_b2, ALU.subtract), deps=[tv1, tv2], token=True)
        tr1 = prog.emit("act", lambda e: e.activation(
            out=st_r2, in_=st_q2, func=AF.Ln, bias=cap["eps"]),
            deps=[tv3, capt["eps"]], token=True)
        tr2 = prog.emit("act", lambda e: e.activation(
            out=st_r2, in_=st_r2, func=AF.Exp, scale=cap["mhalf"]),
            deps=[tr1, capt["mhalf"]], token=True)
        tb = prog.emit("dve", lambda e: e.tensor_tensor(
            st_b2, st_m2, st_r2, ALU.mult), deps=[tm, tr2], token=True)
        tb2 = prog.emit("dve", lambda e: e.tensor_scalar(
            st_b2, st_b2, -1.0, None, ALU.mult), deps=[tb], token=True)
        return [tr2, tb2]

    wo_free = bar_b
    yt_free = [bar_b, bar_b]
    cs_free = [bar_b, bar_b]
    cd_free = bar_b
    wo_last = []
    astat_toks = {}
    att_sp = {}
    for dq in range(4):
        two = prog.dma("pool", ld_wo, wout_q,
                       w_out[dq].rearrange("cb p d -> p cb d"),
                       deps=list(wo_free))
        mm_last = {}
        for st in range(NT):
            sl = st % 2
            tyt = prog.dma("sp", ld_yt[sl],
                           ynt_st[sl].rearrange("c (cb s) -> c cb s", cb=64),
                           scr_ynt[st].rearrange("cb c s -> c cb s"),
                           deps=list(yt_free[sl]) + t_yn_done)
            for cb in range(64):
                for dbh in range(2):
                    tmm = prog.emit("pe", lambda e, sl=sl, cb=cb, dbh=dbh:
                                    e.matmul(
                        pg[:, dbh * 512:(dbh + 1) * 512],
                        ynt_st[sl][:, cb * 128:(cb + 1) * 128],
                        wout_q[:, cb * 1024 + dbh * 512:cb * 1024 +
                               (dbh + 1) * 512],
                        start=(cb == 0), stop=(cb == 63)),
                        deps=([tyt, two] + fr["pg"]) if cb == 0 else [],
                        token=(cb == 63))
                    if cb == 63:
                        mm_last[dbh] = tmm
            tev = prog.emit("dve", lambda e, sl=sl: e.tensor_copy(
                cstage[sl], pg[:, 0:1024]),
                deps=[mm_last[0], mm_last[1]] + list(cs_free[sl]), token=True)
            fr["pg"] = [tev]
            yt_free[sl] = [tev]
            col = dq * 16 + st
            trs = prog.emit("dve", lambda e, sl=sl, col=col: e.reduce_sum(
                stat_s[:, col:col + 1], cstage[sl],
                axis=mybir.AxisListType.X), deps=[tev], token=True)
            tsq = prog.emit("act", lambda e, sl=sl, col=col: e.activation(
                out=cdump[:, 0:1024], in_=cstage[sl], func=AF.Square,
                accum_out=stat_q[:, col:col + 1]),
                deps=[tev] + list(cd_free), token=True)
            cd_free = [tsq]
            astat_toks[(dq, st)] = [trs, tsq]
            tsp = prog.dma("sp", sp_att[sl],
                           scr_att[st * 128:(st + 1) * 128,
                                   dq * 1024:(dq + 1) * 1024],
                           cstage[sl], deps=[trs, tsq])
            cs_free[sl] = [tsp]
            att_sp[(dq, st)] = tsp
        wo_free = [mm_last[1]]
        wo_last = [mm_last[1]]

    t_att_done = [sp_att[0].tok(), sp_att[1].tok()]
    epb = [A.at32(wout_off + k * D, D) for k in range(4)]
    ep_free = [list(wo_last) + t_att_done for _ in range(4)]
    x1stat_toks = {}
    for st in range(NT):
        sl = st % 2
        ba, bx = epb[sl], epb[2 + sl]
        ta = prog.dma("sp", ld_ep[sl], ba,
                      scr_att[st * 128:(st + 1) * 128, :],
                      deps=ep_free[sl] + [att_sp[(q, st)] for q in range(4)])
        tx = prog.dma("act", ld_ep[2 + sl], bx,
                      xin[128 + st * 128:128 + (st + 1) * 128, :],
                      deps=ep_free[2 + sl])
        sdeps = sum((astat_toks[(q, st)] for q in range(4)), [])
        tfs = prog.emit("dve", lambda e, st=st: e.reduce_sum(
            st_m2, stat_s[:, st:st + 49:16], axis=mybir.AxisListType.X),
            deps=sdeps + (coef_free[0] or []), token=True)
        tfq = prog.emit("dve", lambda e, st=st: e.reduce_sum(
            st_q2, stat_q[:, st:st + 49:16], axis=mybir.AxisListType.X),
            deps=sdeps + (coef_free[0] or []), token=True)
        cf = emit_ln_coef(st_m2, st_q2, [tfs, tfq])
        tt1 = prog.emit("dve", lambda e, ba=ba: e.tensor_scalar(
            ba, ba, st_r2, st_b2, ALU.mult, ALU.add),
            deps=[ta] + cf, token=True)
        coef_free[0] = [tt1]
        tx1 = prog.emit("pool", lambda e, ba=ba, bx=bx: e.tensor_tensor(
            ba, ba, bx, ALU.add), deps=[tt1, tx], token=True)
        txs = prog.emit("dve", lambda e, ba=ba, st=st: e.reduce_sum(
            xstat_s[:, st:st + 1], ba, axis=mybir.AxisListType.X),
            deps=[tx1], token=True)
        txq = prog.emit("act", lambda e, ba=ba, st=st: e.activation(
            out=cdump, in_=ba, func=AF.Square,
            accum_out=xstat_q[:, st:st + 1]), deps=[tx1] + list(cd_free),
            token=True)
        cd_free = [txq]
        x1stat_toks[st] = [txs, txq]
        tspx = prog.dma("sp", sp_x1[sl], scr_x1[st * 128:(st + 1) * 128, :],
                        ba, deps=[tx1, txs, txq])
        ep_free[sl] = [tspx]
        ep_free[2 + sl] = [tx1]
    t_x1_done = [sp_x1[0].tok(), sp_x1[1].tok()]

    ln2st = [A.at16(wout_off + 4 * D + k * (D // 2), D) for k in range(2)]
    l2_free = [[], []]
    for st in range(NT):
        sl = st % 2
        ba = epb[sl]
        ta = prog.dma("sp", ld_ep[sl], ba,
                      scr_x1[st * 128:(st + 1) * 128, :],
                      deps=ep_free[sl] + t_x1_done)
        cf = emit_ln_coef(xstat_s[:, st:st + 1], xstat_q[:, st:st + 1],
                          x1stat_toks[st])
        tap = prog.emit("act", lambda e, ba=ba, sl=sl: e.activation(
            out=ln2st[sl], in_=ba, func=AF.Identity, scale=st_r2, bias=st_b2),
            deps=[ta] + cf + l2_free[sl], token=True)
        coef_free[0] = [tap]
        tsp = prog.dma("act", sp_ln2[sl], scr_ln2[st * 128:(st + 1) * 128, :],
                       ln2st[sl], deps=[tap])
        l2_free[sl] = [tsp]
        ep_free[sl] = [tap]
    t_ln2_done = [sp_ln2[0].tok(), sp_ln2[1].tok()]

    dbg["arena_used_C"] = A.off
    if stop_after == "C":
        finish(prog, t_ln2_done + t_x1_done)
        return {"dram": dram, "dbg": dbg, "inputs": in_names, "outputs": out_names}

    bar_c = prog.barrier()
    A.reset("preB")
    ffn_off = A.tell()
    h_aug = A.a16(HKB * S)
    z_off = A.tell()
    z_aug = A.a16(ZKB * S)
    lf = [A.a16(S) for _ in range(2)]
    w13t = [A.a16(ZKB * 128) for _ in range(4)]
    a13_s = A.a16(128)
    a2_s = [A.a16(2) for _ in range(2)]
    tmp1 = [A.a16(512) for _ in range(2)]
    tmp3 = [A.a16(512) for _ in range(2)]
    tmph = A.a16(1024)
    _pad = HKB * 1024 // 2 - (A.tell() - z_off)
    if _pad > 0:
        A.a32(_pad)
    w2res = A.at16(z_off, HKB * 1024)
    fdump = A.a16(1024)
    fst = [A.a32(1024) for _ in range(2)]
    fstat_s = A.a32(64)
    fstat_q = A.a32(64)
    ld_lf = [prog.dsem(f"ld_lf{i}") for i in range(2)]
    ld_w13 = [prog.dsem(f"ld_w13{i}") for i in range(4)]
    ld_a2 = [prog.dsem(f"ld_a2{i}") for i in range(2)]
    ld_a13 = prog.dsem("ld_a13")
    ld_w2 = prog.dsem("ld_w2")
    sp_ffn = [prog.dsem(f"sp_ffn{i}") for i in range(2)]

    t_a13 = prog.dma("pool", ld_a13, a13_s.rearrange("p (db f) -> p db f", f=4),
                     a13t.rearrange("db p f -> p db f"), deps=bar_c)
    NSG = S // 512 if S >= 512 else 1
    SGW = min(512, S)
    lf_free = [bar_c, bar_c]
    zw_free = bar_c
    apj_banks = [pg[:, 0:512], pg[:, 512:1024], py6[:, 0:512], py7[:, 0:512]]
    apj_last = [None] * 4
    zfold_toks = {}
    for db in range(NDB):
        sl = db % 2
        tlf = prog.dmat("act", ld_lf[sl], lf[sl],
                        scr_ln2[:, db * 128:(db + 1) * 128],
                        deps=list(lf_free[sl]) + t_ln2_done)
        jb = db % 8
        if db < 8:
            tz = prog.emit("pool", lambda e, sl=sl, jb=jb: e.tensor_copy(
                z_aug[:, jb * S:(jb + 1) * S], lf[sl]),
                deps=[tlf] + ([zw_free] if not isinstance(zw_free, list)
                              else zw_free), token=True)
        else:
            tz = prog.emit("pool", lambda e, sl=sl, jb=jb: e.tensor_tensor(
                z_aug[:, jb * S:(jb + 1) * S], z_aug[:, jb * S:(jb + 1) * S],
                lf[sl], ALU.add), deps=[tlf, zfold_toks[(db - 8, jb)]],
                token=True)
        zfold_toks[(db, jb)] = tz
        for sg in range(NSG):
            ap_l = prog.emit("pe", lambda e, sl=sl, db=db, sg=sg: e.matmul(
                apj_banks[sg][0:4, 0:SGW],
                a13_s[:, db * 4:(db + 1) * 4],
                lf[sl][:, sg * SGW:(sg + 1) * SGW],
                start=(db == 0), stop=(db == NDB - 1)),
                deps=([tlf, t_a13] + (fr["pg"] + fr["py6"] + fr["py7"]
                                      if db == 0 and sg == 0 else [])),
                token=True)
            if db == NDB - 1:
                apj_last[sg] = ap_l
        lf_free[sl] = [tz, ap_l]
    tzm = prog.emit("dve", lambda e: e.memset(z_aug[:, 8 * S:9 * S], 0.0),
                    deps=bar_c, token=True)
    za_toks = []
    for sg in range(NSG):
        tza = prog.emit("act", lambda e, sg=sg: e.activation(
            out=z_aug.rearrange("p (kb s) -> p kb s", kb=ZKB)
            [0:4, 8, sg * SGW:(sg + 1) * SGW],
            in_=apj_banks[sg][0:4, 0:SGW], func=AF.Identity,
            scale=cap["half"][0:4, :]),
            deps=[apj_last[sg], tzm, capt["half"]], token=True)
        za_toks.append(tza)
    fr["pg"] = za_toks[:2]
    fr["py6"] = [za_toks[2]] if NSG > 2 else [tzm]
    fr["py7"] = [za_toks[3]] if NSG > 3 else [tzm]

    t_hm = prog.emit("dve", lambda e: e.memset(
        h_aug[:, 21 * S:23 * S], 0.0), deps=bar_c, token=True)
    zfold_final = [zfold_toks[(db, db % 8)] for db in range(max(0, NDB - 8), NDB)]
    hfold_toks = {}
    w13_free = [[t_hm] for _ in range(4)]
    a2_free = [[t_hm], [t_hm]]
    t1_free = [[t_hm], [t_hm]]
    t3_free = [[t_hm], [t_hm]]
    th_free = [[t_hm], [t_hm]]
    a2h_last = [None] * 4
    pgh_free = [list(za_toks[0:1]) or [t_hm], list(za_toks[1:2]) or [t_hm]]
    py_free = [fr["py6"], fr["py7"]]
    NS2 = max(1, S // 512)
    SG2 = min(512, S)
    NSH = (NS2 + 1) // 2
    first_mm = True
    for seg in range(4):
        for ct in range(SEGT):
            idx = seg * SEGT + ct
            sl = idx % 2
            tw1 = prog.dma("pool", ld_w13[sl], w13t[sl],
                           w1aug[idx].rearrange("kb p c -> p kb c"),
                           deps=list(w13_free[sl]))
            tw3 = prog.dma("pool", ld_w13[2 + sl], w13t[2 + sl],
                           w3aug[idx].rearrange("kb p c -> p kb c"),
                           deps=list(w13_free[2 + sl]))
            ta2 = prog.dma("pool", ld_a2[sl], a2_s[sl],
                           a2t[idx], deps=list(a2_free[sl]))
            last_a2h = None
            for sh in range(NSH):
                sgs = [g for g in (2 * sh, 2 * sh + 1) if g < NS2]
                h1_last = {}
                h3_last = {}
                for kb in range(ZKB):
                    for g in sgs:
                        sp2 = g % 2
                        dd = []
                        if kb == 0:
                            dd = [tw1] + pgh_free[sp2]
                            if first_mm:
                                dd += za_toks + zfold_final
                        h1_last[g] = prog.emit(
                            "pe", lambda e, sl=sl, kb=kb, g=g, sp2=sp2: e.matmul(
                                pg[:, sp2 * 512:sp2 * 512 + SG2],
                                w13t[sl][:, kb * 128:(kb + 1) * 128],
                                z_aug[:, kb * S + g * SG2:kb * S + (g + 1) * SG2],
                                start=(kb == 0), stop=(kb == ZKB - 1)),
                            deps=dd, token=(kb == ZKB - 1))
                        first_mm = False
                for kb in range(ZKB):
                    for g in sgs:
                        sp2 = g % 2
                        pyb = py6 if sp2 == 0 else py7
                        dd = [tw3] + py_free[sp2] if kb == 0 else []
                        h3_last[g] = prog.emit(
                            "pe", lambda e, sl=sl, kb=kb, g=g, pyb=pyb: e.matmul(
                                pyb[:, 0:SG2],
                                w13t[2 + sl][:, kb * 128:(kb + 1) * 128],
                                z_aug[:, kb * S + g * SG2:kb * S + (g + 1) * SG2],
                                start=(kb == 0), stop=(kb == ZKB - 1)),
                            deps=dd, token=(kb == ZKB - 1))
                for g in sgs:
                    sp2 = g % 2
                    pyb = py6 if sp2 == 0 else py7
                    te1 = prog.emit("act", lambda e, sp2=sp2: e.activation(
                        out=tmp1[sp2][:, 0:SG2],
                        in_=pg[:, sp2 * 512:sp2 * 512 + SG2], func=AF.Silu),
                        deps=[h1_last[g]] + list(t1_free[sp2]), token=True)
                    pgh_free[sp2] = [te1]
                    te3 = prog.emit("act", lambda e, sp2=sp2, pyb=pyb:
                                    e.activation(
                        out=tmp3[sp2][:, 0:SG2], in_=pyb[:, 0:SG2],
                        func=AF.Copy), deps=[h3_last[g]] + list(t3_free[sp2]),
                        token=True)
                    py_free[sp2] = [te3]
                    thm = prog.emit("dve", lambda e, sp2=sp2: e.tensor_tensor(
                        tmph[:, sp2 * 512:sp2 * 512 + SG2],
                        tmp1[sp2][:, 0:SG2], tmp3[sp2][:, 0:SG2], ALU.mult),
                        deps=[te1, te3] + list(th_free[sp2]), token=True)
                    t1_free[sp2] = [thm]
                    t3_free[sp2] = [thm]
                    dst = h_aug[:, ct * S + g * SG2:ct * S + (g + 1) * SG2]
                    if seg == 0:
                        tf = prog.emit("pool", lambda e, dst=dst, sp2=sp2:
                                       e.tensor_copy(
                            dst, tmph[:, sp2 * 512:sp2 * 512 + SG2]),
                            deps=[thm] + ([t_hm] if ct >= 21 else []),
                            token=True)
                    else:
                        tf = prog.emit("pool", lambda e, dst=dst, sp2=sp2:
                                       e.tensor_tensor(
                            dst, dst, tmph[:, sp2 * 512:sp2 * 512 + SG2],
                            ALU.add), deps=[thm, hfold_toks[(seg - 1, ct, g)]],
                            token=True)
                    hfold_toks[(seg, ct, g)] = tf
                    a2b = pq[g // 2][0:2, (g % 2) * 512:(g % 2) * 512 + SG2]
                    ta2h = prog.emit("pe", lambda e, a2b=a2b, sl=sl, sp2=sp2,
                                     idx=idx: e.matmul(
                        a2b, a2_s[sl][:, 0:2],
                        tmph[:, sp2 * 512:sp2 * 512 + SG2],
                        start=(idx == 0), stop=(idx == 4 * SEGT - 1)),
                        deps=[ta2, thm] +
                        (fr[f"pq{g // 2}"] if idx == 0 else []),
                        token=True)
                    if idx == 4 * SEGT - 1:
                        a2h_last[g] = ta2h
                    th_free[sp2] = [ta2h, tf]
                    last_a2h = ta2h
            w13_free[sl] = [h1_last[sgs[-1]]]
            w13_free[2 + sl] = [h3_last[sgs[-1]]]
            a2_free[sl] = [last_a2h]
    a2h_toks = []
    for g in range(NS2):
        tza = prog.emit("act", lambda e, g=g: e.activation(
            out=h_aug.rearrange("p (kb s) -> p kb s", kb=HKB)
            [0:2, 22, g * SG2:(g + 1) * SG2],
            in_=pq[g // 2][0:2, (g % 2) * 512:(g % 2) * 512 + SG2],
            func=AF.Identity, scale=cap["half"][0:2, :]),
            deps=[a2h_last[g], t_hm, capt["half"]], token=True)
        a2h_toks.append(tza)
        fr[f"pq{g // 2}"] = [tza]

    bar_f1 = prog.barrier()
    w2_free = bar_f1
    fs_free = [bar_f1, bar_f1]
    fd_free = bar_f1
    fstat_toks = {}
    for dq in range(4):
        tw2 = prog.dma("pool", ld_w2, w2res,
                       w2aug[dq].rearrange("kb p d -> p kb d"),
                       deps=list(w2_free))
        mm_last = {}
        for st in range(NT):
            sl = st % 2
            for kb in range(HKB):
                for dbh in range(2):
                    tmm = prog.emit("pe", lambda e, st=st, kb=kb, dbh=dbh:
                                    e.matmul(
                        pg[:, dbh * 512:(dbh + 1) * 512],
                        h_aug[:, kb * S + st * 128:kb * S + (st + 1) * 128],
                        w2res[:, kb * 1024 + dbh * 512:kb * 1024 +
                              (dbh + 1) * 512],
                        start=(kb == 0), stop=(kb == HKB - 1)),
                        deps=([tw2] + a2h_toks + bar_f1 + fr["pg"])
                        if kb == 0 else [],
                        token=(kb == HKB - 1))
                    if kb == HKB - 1:
                        mm_last[dbh] = tmm
            tev = prog.emit("dve", lambda e, sl=sl: e.tensor_copy(
                fst[sl], pg[:, 0:1024]),
                deps=[mm_last[0], mm_last[1]] + list(fs_free[sl]), token=True)
            fr["pg"] = [tev]
            col = dq * 16 + st
            trs = prog.emit("dve", lambda e, sl=sl, col=col: e.reduce_sum(
                fstat_s[:, col:col + 1], fst[sl],
                axis=mybir.AxisListType.X), deps=[tev], token=True)
            tsq = prog.emit("act", lambda e, sl=sl, col=col: e.activation(
                out=fdump, in_=fst[sl], func=AF.Square,
                accum_out=fstat_q[:, col:col + 1]),
                deps=[tev] + list(fd_free), token=True)
            fd_free = [tsq]
            fstat_toks[(dq, st)] = [trs, tsq]
            tsp = prog.dma("sp", sp_ffn[sl],
                           scr_ffn[st * 128:(st + 1) * 128,
                                   dq * 1024:(dq + 1) * 1024],
                           fst[sl], deps=[trs, tsq])
            fs_free[sl] = [tsp]
        w2_free = [mm_last[1]]

    t_ffn_done = [sp_ffn[0].tok(), sp_ffn[1].tok()]
    bar_f2 = prog.barrier()
    fb1 = [A.at32(ffn_off + k * D, D) for k in range(2)]
    fb2 = [A.at32(ffn_off + (2 + k) * D, D) for k in range(2)]
    fe_free = [bar_f2, bar_f2]
    fx_free = [bar_f2, bar_f2]
    out_toks = []
    for st in range(NT):
        sl = st % 2
        ba, bx = fb1[sl], fb2[sl]
        ta = prog.dma("sp", ld_ep[sl], ba,
                      scr_ffn[st * 128:(st + 1) * 128, :],
                      deps=list(fe_free[sl]) + t_ffn_done)
        tx = prog.dma("act", ld_ep[2 + sl], bx,
                      scr_x1[st * 128:(st + 1) * 128, :],
                      deps=list(fx_free[sl]) + t_x1_done)
        sdeps = sum((fstat_toks[(q, st)] for q in range(4)), [])
        tfs = prog.emit("dve", lambda e, st=st: e.reduce_sum(
            st_m2, fstat_s[:, st:st + 49:16], axis=mybir.AxisListType.X),
            deps=sdeps + (coef_free[0] or []), token=True)
        tfq = prog.emit("dve", lambda e, st=st: e.reduce_sum(
            st_q2, fstat_q[:, st:st + 49:16], axis=mybir.AxisListType.X),
            deps=sdeps + (coef_free[0] or []), token=True)
        cf = emit_ln_coef(st_m2, st_q2, [tfs, tfq])
        tt1 = prog.emit("dve", lambda e, ba=ba: e.tensor_scalar(
            ba, ba, st_r2, st_b2, ALU.mult, ALU.add),
            deps=[ta] + cf, token=True)
        coef_free[0] = [tt1]
        tof = prog.emit("pool", lambda e, ba=ba, bx=bx: e.tensor_tensor(
            ba, ba, bx, ALU.add), deps=[tt1, tx], token=True)
        tsp = prog.dma("sp", sp_out[sl], out_hbm[st * 128:(st + 1) * 128, :],
                       ba, deps=[tof])
        out_toks.append(tsp)
        fe_free[sl] = [tsp]
        fx_free[sl] = [tof]
    dbg["arena_used"] = A.off

    finish(prog, [sp_out[0].tok(), sp_out[1].tok()])
    return {"dram": dram, "dbg": dbg, "inputs": in_names, "outputs": out_names}



def finish(prog, toks):
    toks = [t for t in toks if t and t is not Prog.SKIP]
    prog.recs["sp"].append(Rec(lambda e: None, tuple(toks), None, 0))
    with prog.nc.Block() as block:
        prog.run(block)



def prep_shared(inp):
    f32 = np.float32
    w = {}
    ipw = np.ascontiguousarray(inp["L0_attn_in_proj_w"], f32)
    w["w_in"] = np.ascontiguousarray(
        ipw.reshape(NCT, 128, NDB, 128).transpose(0, 3, 2, 1))
    cw = np.ascontiguousarray(inp["L0_attn_conv_w"], f32).reshape(CONVD, KCONV)
    w["convw"] = np.ascontiguousarray(cw.reshape(80, 128, KCONV).transpose(1, 0, 2))
    cb = np.ascontiguousarray(inp["L0_attn_conv_b"], f32)
    w["convb"] = np.ascontiguousarray(cb.reshape(80, 128).T)
    w["dtb"] = np.ascontiguousarray(inp["L0_attn_dt_bias"], f32).reshape(128, 1)
    w["alog"] = np.ascontiguousarray(inp["L0_attn_A_log"], f32).reshape(128, 1)
    Dv = np.ascontiguousarray(inp["L0_attn_D"], f32)
    w["drow"] = Dv.reshape(1, 128)
    nw = np.ascontiguousarray(inp["L0_attn_norm_w"], f32)
    w["nwfm"] = np.ascontiguousarray(nw.reshape(64, 128).T)
    w["impulse"] = np.ascontiguousarray(np.tile(np.eye(128, dtype=f32), (1, 8)))
    opw = np.ascontiguousarray(inp["L0_attn_out_proj_w"], f32)
    w["w_out"] = np.ascontiguousarray(
        opw.T.reshape(64, 128, 4, 1024).transpose(2, 0, 1, 3))

    def augw(wq, bT_first, bT):
        k0 = wq.shape[1]
        Wt = np.zeros((ZP, FFH), f32)
        Wt[:k0] = np.ascontiguousarray(wq, f32).T
        if bT_first:
            Wt[k0:k0 + 2] = np.ascontiguousarray(bT, f32).T
        else:
            Wt[k0 + 2:k0 + 4] = np.ascontiguousarray(bT, f32).T
        tiles = np.zeros((4 * SEGT, ZP, 128), f32)
        for seg in range(4):
            for t in range(SEGT):
                c0 = seg * FFQ + t * 128
                c1 = min(seg * FFQ + FFQ, c0 + 128)
                c1 = min(c1, FFH)
                if c1 > c0:
                    tiles[seg * SEGT + t, :, 0:c1 - c0] = Wt[:, c0:c1]
        return np.ascontiguousarray(
            tiles.reshape(4 * SEGT, ZKB, 128, 128))

    w["w1aug"] = augw(inp["L0_ffn_w1q"], True, inp["L0_ffn_b1"])
    w["w3aug"] = augw(inp["L0_ffn_w3q"], False, inp["L0_ffn_b3"])
    w2 = np.zeros((HP, D), np.float32)
    w2[:FFQ] = np.ascontiguousarray(inp["L0_ffn_w2q"], f32).T
    w2[2816:2818] = np.ascontiguousarray(inp["L0_ffn_b2"], f32).T
    w["w2aug"] = np.ascontiguousarray(
        w2.reshape(HKB, 128, 4, 1024).transpose(2, 0, 1, 3))
    a13 = np.concatenate([np.ascontiguousarray(inp["L0_ffn_a1"], f32).T,
                          np.ascontiguousarray(inp["L0_ffn_a3"], f32).T], axis=1)
    w["a13t"] = np.ascontiguousarray(a13.reshape(NDB, 128, 4))
    a2 = np.ascontiguousarray(inp["L0_ffn_a2"], f32).T
    a2t = np.zeros((4 * SEGT, 128, 2), f32)
    for seg in range(4):
        for t in range(SEGT):
            c0 = seg * FFQ + t * 128
            c1 = min(seg * FFQ + FFQ, c0 + 128)
            if c1 > c0:
                a2t[seg * SEGT + t, 0:c1 - c0] = a2[c0:c1]
    w["a2t"] = a2t
    return w


def prep_core(inp, core, S, n_cores, shared):
    f32 = np.float32
    hs = np.ascontiguousarray(inp["L0_attn_hidden_states"], f32)
    if hs.ndim == 3:
        hs = hs[0]
    xin = np.zeros((S + 128, D), f32)
    lo = core * S
    xin[128:] = hs[lo:lo + S]
    if core > 0:
        xin[125:128] = hs[lo - 3:lo]
    m = np.zeros((128, 8), f32)
    m[:, :core] = 1.0
    d = dict(shared)
    d["xin"] = xin
    d["selmask"] = m
    return d


def kernel(**inputs):
    S = 2048
    n_cores = 8
    cores = list(range(n_cores))
    shared = prep_shared(inputs)
    core_d = [prep_core(inputs, c, S, n_cores, shared) for c in range(n_cores)]

    nc1 = bass.Bass("TRN2", target_bir_lowering=False, debug=False)
    b1 = build_program(nc1, S, n_cores, stop_after="full", launch="l1")
    nc1.finalize()
    maps1 = [{k: core_d[c].get(k, shared.get(k)) for k in b1["inputs"]}
             for c in range(n_cores)]
    res1 = run_bass_kernel_spmd(nc1, maps1, core_ids=cores)

    stall = np.concatenate([np.asarray(res1.results[c]["scr_stx"])
                            for c in range(n_cores)], axis=0)

    nc2 = bass.Bass("TRN2", target_bir_lowering=False, debug=False)
    b2 = build_program(nc2, S, n_cores, stop_after="full", launch="l2")
    nc2.finalize()
    maps2 = []
    for c in range(n_cores):
        m = {}
        for k in b2["inputs"]:
            if k == "scr_stall":
                m[k] = stall
            elif k in ("scr_hbc", "scr_sgn", "dtraw_io"):
                m[k] = np.asarray(res1.results[c][k])
            else:
                m[k] = core_d[c].get(k, shared.get(k))
        maps2.append(m)
    res2 = run_bass_kernel_spmd(nc2, maps2, core_ids=cores)
    out = np.concatenate([np.asarray(res2.results[c]["out"])
                          for c in range(n_cores)], axis=0)
    return out.reshape(1, n_cores * S, D).astype(np.float32)
